# version 09
# baseline (speedup 1.0000x reference)
import math
import contextlib
import numpy as np
import ml_dtypes
import concourse.bass as bass
import concourse.mybir as mybir
from concourse.bass_utils import run_bass_kernel_spmd

F32 = mybir.dt.float32
BF16 = mybir.dt.bfloat16
U8 = mybir.dt.uint8
AF = mybir.ActivationFunctionType
ALU = mybir.AluOpType
AX = mybir.AxisListType

D = 4096
CTX = 2048
NQ = 1024
NH = 32
HD = 128
NT = CTX // 128
QT = NQ // 128
NC = 32
SCALE = 1.0 / math.sqrt(128.0)
SHIFT = -12.0
NEGB = -30000.0
EPS = 1e-5

ENGINES = ("tensor", "vector", "scalar", "sync", "gpsimd")
ESEM = {"tensor": "t", "vector": "v", "scalar": "sc"}


class Res:
    __slots__ = ("name", "w", "r")

    def __init__(self, name):
        self.name = name
        self.w = None
        self.r = {}


SERIAL = False
SERIAL_PHASES = {"G"}


class Builder:
    def __init__(self):
        self.phase = "P0"
        self.ops = []
        self.cnt = {}
        self.waited = {e: {} for e in ENGINES}
        self.pend = {e: {"r": [], "w": []} for e in ENGINES}
        self.bar = []

    def val(self, sem):
        return self.cnt.get(sem, 0)

    def _bump(self, sem, amt):
        self.cnt[sem] = self.cnt.get(sem, 0) + amt
        return self.cnt[sem]

    def barrier(self):
        self.bar = [(s, v) for s, v in self.cnt.items()]

    def op(self, eng, fn, reads=(), writes=(), inc=None, sreads=()):
        if SERIAL or self.phase in SERIAL_PHASES:
            if not SERIAL and self.phase == "G":
                self.bar = [(s, v) for s, v in self.cnt.items()
                            if s in ("t", "v", "sc")]
            else:
                self.bar = [(s, v) for s, v in self.cnt.items()]
            if inc is None:
                inc = (ESEM.get(eng, "sx"), 1) if eng in ESEM else None
            sreads = ()
            waits0 = list(self.bar)
            for r0 in list(reads) + list(writes):
                if r0.w:
                    waits0.append(r0.w)
            best0 = {}
            for s, v in waits0:
                if v > best0.get(s, -1):
                    best0[s] = v
            wd0 = self.waited[eng]
            out_w = []
            for s, v in best0.items():
                if wd0.get(s, -1) >= v:
                    continue
                wd0[s] = v
                out_w.append((s, v))
            incs = []
            if inc is not None:
                sem, amt = inc
                val = self._bump(sem, amt)
                incs.append((sem, amt))
                for r0 in list(reads):
                    if r0.r.get(sem, -1) < val:
                        r0.r[sem] = val
                for r0 in list(writes):
                    r0.w = (sem, val)
                    r0.r = {}
            self.ops.append((eng, fn, out_w, incs))
            return
        waits = list(self.bar)
        for res in reads:
            if res.w:
                waits.append(res.w)
        hard_sems = set()
        for res in sreads:
            if res.w:
                waits.append(res.w)
                hard_sems.add(res.w[0])
        for res in writes:
            if res.w:
                waits.append(res.w)
            waits.extend(res.r.items())
        own = ESEM.get(eng)
        wd = self.waited[eng]
        best = {}
        for s, v in waits:
            if v > best.get(s, -1):
                best[s] = v
        out_w = []
        for s, v in best.items():
            if s == own and s not in hard_sems:
                continue
            if wd.get(s, -1) >= v:
                continue
            wd[s] = v
            out_w.append((s, v))
        incs = []
        if inc is not None:
            sem, amt = inc
            val = self._bump(sem, amt)
            incs.append((sem, amt))
            for res in list(reads) + list(sreads) + self.pend[eng]["r"]:
                if res.r.get(sem, -1) < val:
                    res.r[sem] = val
            for res in list(writes) + self.pend[eng]["w"]:
                res.w = (sem, val)
                res.r = {}
            self.pend[eng] = {"r": [], "w": []}
        else:
            self.pend[eng]["r"].extend(reads)
            self.pend[eng]["r"].extend(sreads)
            self.pend[eng]["w"].extend(writes)
        self.ops.append((eng, fn, out_w, incs))


def build_program(h, debug=False):
    nc = bass.Bass("TRN2", target_bir_lowering=False, debug=False)

    dt = nc.dram_tensor
    x_d = dt("x", [CTX, D], F32, kind="ExternalInput").ap()
    wq_d = dt("wq", [NH, 128, D], F32, kind="ExternalInput").ap()
    wk_d = dt("wk", [NH, 128, D], F32, kind="ExternalInput").ap()
    wv_d = dt("wv", [NH, 128, D], F32, kind="ExternalInput").ap()
    wo_d = dt("wo", [128, NC, D], F32, kind="ExternalInput").ap()
    ra_d = dt("ra", [128, NC, 68], F32, kind="ExternalInput").ap()
    bc_d = dt("bc", [64, D], F32, kind="ExternalInput").ap()
    cs_d = dt("cs", [128, 2 * CTX], F32, kind="ExternalInput").ap()
    idb_d = dt("identb", [128, 128], BF16, kind="ExternalInput").ap()
    idf_d = dt("identf", [128, 128], F32, kind="ExternalInput").ap()
    psw_d = dt("pswap", [128, 128], BF16, kind="ExternalInput").ap()
    one_d = dt("onesb", [128, 128], BF16, kind="ExternalInput").ap()
    tri_d = dt("tri", [128, 128], BF16, kind="ExternalInput").ap()
    atr_d = dt("atri", [128, 128], BF16, kind="ExternalInput").ap()
    exb_d = dt("expb", [128, NT], F32, kind="ExternalInput").ap()

    ik = "ExternalOutput" if debug else "Internal"
    xmid_d = dt("xmid", [CTX, D], F32, kind=ik).ap()
    q_d = dt("qh", [NH, 128, NQ], BF16, kind=ik).ap()
    k_d = dt("kh", [NH, 128, CTX], BF16, kind=ik).ap()
    v_d = dt("vh", [NH, 128, CTX], BF16, kind=ik).ap()
    att_d = dt("atth", [NH, 128, NQ], BF16, kind=ik).ap()
    xpost_d = dt("xpost", [NQ, D], F32, kind=ik).ap()
    rec_d = dt("recb", [1, NQ], F32, kind="Internal").ap()
    out_d = dt("out", [NQ, D], F32, kind="ExternalOutput").ap()
    if debug:
        dstt_d = dt("dstt", [NT, 128, 96], F32, kind="ExternalOutput").ap()
        dxn1_d = dt("dxn1", [NT, 128, D], BF16, kind="ExternalOutput").ap()
        dgat_d = dt("dgat", [NT, 128, 4], BF16, kind="ExternalOutput").ap()
        dxnt_d = dt("dxnt", [NT, 128, D], BF16, kind="ExternalOutput").ap()
        dpb_d = dt("dpb", [128, 10240], BF16, kind="ExternalOutput").ap()
        dden_d = dt("dden", [NH, 1, 1024], F32, kind="ExternalOutput").ap()
        drec_d = dt("drec", [NH, 128, 1024], F32, kind="ExternalOutput").ap()
        dlgb_d = dt("dlgb", [NT, 4, 128], F32, kind="ExternalOutput").ap()
        dgsc_d = dt("dgsc", [NT, 128, 16], F32, kind="ExternalOutput").ap()

    ABYTES = 63488

    with contextlib.ExitStack() as ctx:
        E = ctx.enter_context
        big0 = E(nc.sbuf_tensor("big0", [128, 131072], U8))
        aren = E(nc.sbuf_tensor("aren", [128, ABYTES], U8))
        cs_s = E(nc.sbuf_tensor("cs_s", [128, 2 * CTX], F32))
        idb = E(nc.sbuf_tensor("idb", [128, 128], BF16))
        idf = E(nc.sbuf_tensor("idf", [128, 128], F32))
        psw = E(nc.sbuf_tensor("psw", [128, 128], BF16))
        oneb = E(nc.sbuf_tensor("oneb", [128, 128], BF16))
        trib = E(nc.sbuf_tensor("trib", [128, 128], BF16))
        atrb = E(nc.sbuf_tensor("atrb", [128, 128], BF16))
        exbs = E(nc.sbuf_tensor("exbs", [128, NT], F32))
        epsb = E(nc.sbuf_tensor("epsb", [128, 1], F32))
        pAB = E(nc.psum_tensor("pAB", [128, 1024], F32))
        pCD = E(nc.psum_tensor("pCD", [128, 1024], F32))
        pE = E(nc.psum_tensor("pE", [128, 1024], F32))
        pF = E(nc.psum_tensor("pF", [128, 1024], F32))
        sems = {n: E(nc.semaphore(n)) for n in ("t", "v", "sc", "sx", "sg", "sb")}
        block = E(nc.Block())

        B = Builder()
        R = {}

        def res(name):
            if name not in R:
                R[name] = Res(name)
            return R[name]

        def av(off, n, dtype):
            sz = 2 if dtype == BF16 else 4
            return aren[:, off:off + n * sz].bitcast(dtype)

        big_bf = big0[:, :].bitcast(BF16)
        xn2v = big_bf.rearrange("p (c t) -> p c t", c=NC)

        xr0 = av(0, 4096, F32)
        xr1 = av(16384, 4096, F32)
        xntm = av(32768, 4096, BF16)
        xnfm = av(40960, 4096, BF16)
        rAs = av(49152, 2176, BF16)
        bcs = av(53504, 4096, BF16)
        smidb = av(61696, 128, BF16)
        lgb = av(61952, 128, F32)
        gsc = av(62464, 32, F32)
        gbf = av(62592, 4, BF16)
        g64t = av(62720, 64, BF16)
        g64s = av(62848, 128, BF16)
        stt = av(63104, 96, F32)
        wqs = [av(0, 4096, BF16), av(8192, 4096, BF16)]
        wks = [av(16384, 4096, BF16), av(24576, 4096, BF16)]
        wvs = av(32768, 4096, BF16)
        q0 = av(40960, 1024, BF16)
        k0 = av(43008, 2048, BF16)
        v0 = av(47104, 2048, BF16)
        vfm = av(51200, 2048, BF16)
        rtmp = av(55296, 2048, BF16)
        ks = [av(0, 2048, BF16), av(4096, 2048, BF16)]
        vs = [av(8192, 2048, BF16), av(12288, 2048, BF16),
              av(36864, 2048, BF16)]
        qs = [av(16384, 1024, BF16), av(18432, 1024, BF16)]
        rec16 = av(40960, 1024, BF16)
        pb = [big_bf[:, 0:10240], big_bf[:, 10240:20480]]
        atts = [av(20480, 1024, BF16), av(22528, 1024, BF16)]
        recb = av(24576, 1024, F32)
        denr = av(28672, 1024, F32)
        recr = av(32768, 1024, F32)
        xms = [av(0, 512, F32), av(2048, 512, F32)]
        xps = [av(4096, 512, F32), av(6144, 512, F32)]
        woH = [big_bf[:, 0:16384].rearrange("p (c o) -> p c o", c=NC),
               big_bf[:, 16384:32768].rearrange("p (c o) -> p c o", c=NC)]
        attV = big_bf[:, 32768:65536].rearrange("p (c q) -> p c q", c=NC)

        o = B.op

        def ld(eng, sem, out, in_, wres):
            o(eng, lambda e, a=out, b=in_: e.dma_start(out=a, in_=b),
              writes=[wres], inc=(sem, 16))

        ld("sync", "sx", cs_s[:, :], cs_d, res("cs"))
        ld("sync", "sx", idb[:, :], idb_d, res("idb"))
        ld("sync", "sx", idf[:, :], idf_d, res("idf"))
        ld("sync", "sx", psw[:, :], psw_d, res("psw"))
        ld("sync", "sx", oneb[:, :], one_d, res("oneb"))
        ld("sync", "sx", trib[:, :], tri_d, res("tri"))
        ld("sync", "sx", atrb[:, :], atr_d, res("atri"))
        ld("sync", "sx", exbs[:, :], exb_d, res("exb"))
        ld("gpsimd", "sg", rAs[:, :], ra_d.rearrange("p c e -> p (c e)"), res("rAs"))
        ld("gpsimd", "sg", bcs[0:64, :], bc_d, res("bcs"))
        o("vector", lambda e: e.memset(epsb[:, 0:1], EPS), writes=[res("eps")],
          inc=("v", 1))

        s0 = stt[:, 0:1]; q1 = stt[:, 1:2]
        mcol = stt[:, 48:49]
        vvar = stt[:, 49:50]; msq = stt[:, 50:51]
        rstd = stt[:, 51:52]; lnv = stt[:, 52:53]

        def ln_stats(xr, rx, junk=None, rjunk=None):
            o("vector", lambda e: e.reduce_sum(s0, xr, axis=AX.X),
              reads=[rx], writes=[res("stt")])
            o("vector", lambda e: e.tensor_tensor(xnfm[:, :], xr, xr, ALU.mult),
              reads=[rx], writes=[res("xnfm")], inc=("v", 1))
            o("vector", lambda e: e.reduce_sum(q1, xnfm[:, :], axis=AX.X),
              reads=[res("xnfm")], writes=[res("stt")])
            ph1 = B.phase
            B.phase = "G"
            o("vector", lambda e: e.tensor_scalar(mcol, s0, 1.0 / D, None,
                                                  ALU.mult),
              writes=[res("stt")], inc=("v", 1))
            o("vector", lambda e: e.tensor_tensor(msq, mcol, mcol, ALU.mult),
              writes=[res("stt")], inc=("v", 1))
            o("vector", lambda e: e.tensor_scalar(vvar, q1, 1.0 / D, None,
                                                  ALU.mult),
              writes=[res("stt")], inc=("v", 1))
            o("vector", lambda e: e.tensor_tensor(vvar, vvar, msq,
                                                  ALU.subtract),
              writes=[res("stt")], inc=("v", 1))
            o("scalar", lambda e: e.activation(lnv, vvar, AF.Ln, bias=epsb[:, 0:1]),
              reads=[res("stt"), res("eps")], writes=[res("stt")], inc=("sc", 1))
            o("scalar", lambda e: e.activation(rstd, lnv, AF.Exp, scale=-0.5),
              writes=[res("stt")], inc=("sc", 1))
            B.phase = ph1

        def gates_and_smid(rmid, dbg_i=None):
            gls = gsc[:, 0:4]; gi = gsc[:, 4:8]; gm = gsc[:, 8:12]
            gm1 = gsc[:, 12:13]; gm2 = gsc[:, 13:14]
            gden = gsc[:, 14:15]; grec = gsc[:, 15:16]
            glg = gsc[:, 16:20]
            o("vector", lambda e: e.tensor_copy(glg, pCD[:, 0:4]),
              reads=[res("pCD0")], writes=[res("gsc")], inc=("v", 1))
            o("vector", lambda e: e.reduce_max(gm1, glg, axis=AX.X),
              writes=[res("gsc")], inc=("v", 1))
            o("vector", lambda e: e.tensor_scalar(gls, glg, gm1, None,
                                                  ALU.subtract),
              sreads=[res("gsc")], writes=[res("gsc")], inc=("v", 1))
            o("vector", lambda e: e.tensor_scalar(gi, gls, 0.0, None, ALU.is_equal),
              writes=[res("gsc")])
            o("vector", lambda e: e.tensor_scalar(gm, gi, NEGB, None,
                                                  ALU.mult),
              writes=[res("gsc")])
            o("vector", lambda e: e.tensor_tensor(gm, gm, gls, ALU.add),
              writes=[res("gsc")])
            o("vector", lambda e: e.reduce_max(gm2, gm, axis=AX.X),
              writes=[res("gsc")], inc=("v", 1))
            o("vector", lambda e: e.tensor_scalar(gi, gls, gm2, None, ALU.is_ge),
              sreads=[res("gsc")], writes=[res("gsc")], inc=("v", 1))
            o("scalar", lambda e: e.activation(gm, gls, AF.Exp),
              reads=[res("gsc")], writes=[res("gsc")], inc=("sc", 1))
            o("vector", lambda e: e.tensor_tensor(gls, gm, gi, ALU.mult),
              writes=[res("gsc")])
            o("vector", lambda e: e.reduce_sum(gden, gls, axis=AX.X),
              writes=[res("gsc")])
            o("vector", lambda e: e.reciprocal(grec, gden),
              writes=[res("gsc")], inc=("v", 1))
            o("vector", lambda e: e.tensor_scalar(gbf, gls, grec, 2.0,
                                                  ALU.mult, ALU.mult),
              sreads=[res("gsc")], writes=[res("gbf")])
            if debug and dbg_i is not None:
                o("sync", lambda e, i=dbg_i: e.dma_start(
                    out=dgsc_d[i], in_=gsc[:, 0:16]),
                  reads=[res("gsc")], writes=[res("dgsc")], inc=("sx", 16))
            o("vector", lambda e: e.tensor_copy(
                g64t[:, :], gbf.unsqueeze(2).broadcast_to((128, 4, 16))),
              reads=[res("gbf")], writes=[res("g64t")], inc=("v", 1))
            o("tensor", lambda e: e.matmul(
                pF[0:64, 4:132], g64t[:, :], idb[:, :],
                start=True, stop=True),
              reads=[res("g64t"), res("idb")], writes=[res("pF0")], inc=("t", 1))
            o("vector", lambda e: e.tensor_copy(
                g64s[0:64, :], pF[0:64, 4:132]),
              reads=[res("pF0")], writes=[res("g64s")], inc=("v", 1))
            o("vector", lambda e: e.tensor_tensor(
                smidb[0:64, :], pAB[0:64, 0:128], g64s[0:64, :], ALU.mult),
              reads=rmid + [res("g64s")], writes=[res("smid")], inc=("v", 1))

        def ffn_tile(xn_fm, rxn, xr, rxr, deltas_into_xr=True, dbg_i=None):
            for c in range(NC):
                o("tensor", lambda e, c=c: e.matmul(
                    pAB[0:68, 0:128], rAs[:, c * 68:(c + 1) * 68],
                    xn_fm[:, c * 128:(c + 1) * 128],
                    start=(c == 0), stop=(c == NC - 1)),
                  reads=[res("rAs"), rxn],
                  writes=[res("pAB0")] if c == 0 else (),
                  inc=("t", 1) if c == NC - 1 else None)
            ph0 = B.phase
            B.phase = "G"
            for c in range(NC):
                o("tensor", lambda e, c=c: e.matmul(
                    pCD[:, 0:4], xn_fm[:, c * 128:(c + 1) * 128],
                    rAs[:, c * 68 + 64:c * 68 + 68],
                    start=(c == 0), stop=(c == NC - 1)),
                  reads=[res("rAs"), rxn],
                  writes=[res("pCD0")] if c == 0 else (),
                  inc=("t", 1) if c == NC - 1 else None)
            gates_and_smid([res("pAB0")], dbg_i=dbg_i)
            B.phase = ph0
            for ob in range(8):
                pp, rn = (pE[:, 0:512], "pE0") if ob % 2 == 0 else \
                         (pE[:, 512:1024], "pE1")
                o("tensor", lambda e, pp=pp, ob=ob: e.matmul(
                    pp, smidb[0:64, :], bcs[0:64, ob * 512:(ob + 1) * 512],
                    start=True, stop=True),
                  reads=[res("smid"), res("bcs")], writes=[res(rn)], inc=("t", 1))
                o("vector", lambda e, pp=pp, ob=ob: e.tensor_tensor(
                    xr[:, ob * 512:(ob + 1) * 512], xr[:, ob * 512:(ob + 1) * 512],
                    pp, ALU.add),
                  reads=[res(rn), rxr], writes=[rxr], inc=("v", 1))

        def transpose_set(src_bf, rsrc, dst_fn, rdst):
            for g in range(4):
                pt, names = (pF, ("pF0", "pF1")) if g % 2 == 0 else \
                            (pE, ("pE0", "pE1"))
                rs = [res(names[0]), res(names[1])]
                for c8 in range(8):
                    c = g * 8 + c8
                    o("tensor", lambda e, c=c, c8=c8, pt=pt: e.matmul(
                        pt[:, c8 * 128:(c8 + 1) * 128],
                        src_bf[:, c * 128:(c + 1) * 128], idb[:, :],
                        start=True, stop=True),
                      reads=[rsrc, res("idb")], writes=rs if c8 == 0 else (),
                      inc=("t", 1) if c8 == 7 else None)
                o("vector", lambda e, g=g, pt=pt: dst_fn(e, g, pt[:, 0:1024]),
                  reads=rs, writes=[rdst], inc=("v", 1))

        B.phase = "P1"
        B.barrier()
        xrs = [xr0, xr1]
        for i in range(NT):
            xr = xrs[i % 2]
            rxr = res(f"xr{i % 2}")
            o("sync", lambda e, i=i, xr=xr: e.dma_start(
                out=xr, in_=x_d[i * 128:(i + 1) * 128, :]),
              writes=[rxr], inc=("sx", 16))
            ln_stats(xr, rxr)
            if debug:
                o("sync", lambda e, i=i: e.dma_start(
                    out=dstt_d[i], in_=stt[:, :]),
                  reads=[res("stt")], writes=[res("dstt")], inc=("sx", 16))
            o("vector", lambda e, xr=xr: e.tensor_scalar(
                xntm, xr, mcol, rstd, ALU.subtract, ALU.mult),
              reads=[rxr], sreads=[res("stt")], writes=[res("xntm")],
              inc=("v", 1))
            transpose_set(xntm, res("xntm"),
                          lambda e, g, src: e.tensor_copy(
                              xnfm[:, g * 1024:(g + 1) * 1024], src),
                          res("xnfm"))
            if debug:
                o("sync", lambda e, i=i: e.dma_start(
                    out=dxn1_d[i], in_=xnfm[:, :]),
                  reads=[res("xnfm")], writes=[res("dxn1")], inc=("sx", 16))
                o("sync", lambda e, i=i: e.dma_start(
                    out=dxnt_d[i], in_=xntm),
                  reads=[res("xntm")], writes=[res("dxnt")], inc=("sx", 16))
            ffn_tile(xnfm, res("xnfm"), xr, rxr, dbg_i=i)
            if debug:
                o("sync", lambda e, i=i: e.dma_start(
                    out=dgat_d[i], in_=gbf),
                  reads=[res("gbf")], writes=[res("dgat")], inc=("sx", 16))
            o("sync", lambda e, i=i, xr=xr: e.dma_start(
                out=xmid_d[i * 128:(i + 1) * 128, :], in_=xr),
              reads=[rxr], writes=[res("xmid_d")], inc=("sx", 16))
            ln_stats(xr, rxr)
            o("vector", lambda e, xr=xr: e.tensor_scalar(
                xntm, xr, mcol, rstd, ALU.subtract, ALU.mult),
              reads=[rxr], sreads=[res("stt")], writes=[res("xntm")],
              inc=("v", 1))
            transpose_set(xntm, res("xntm"),
                          lambda e, g, src, i=i: e.tensor_copy(
                              xn2v[:, g * 8:(g + 1) * 8, i * 128:(i + 1) * 128],
                              src.rearrange("p (c t) -> p c t", c=8)),
                          res("big0"))

        B.phase = "P2"
        B.barrier()
        ld("gpsimd", "sg", wqs[0][:, :], wq_d[0], res("wq0"))
        ld("gpsimd", "sg", wks[0][:, :], wk_d[0], res("wk0"))
        for hh in range(NH):
            buf = hh % 2
            ld("gpsimd", "sg", wvs[:, :], wv_d[hh], res("wv"))
            for tg in range(4):
                pp, rn = (pAB[:, 0:512], "pAB0") if tg % 2 == 0 else \
                         (pAB[:, 512:1024], "pAB1")
                for c in range(NC):
                    o("tensor", lambda e, pp=pp, c=c, tg=tg, buf=buf: e.matmul(
                        pp, wks[buf][:, c * 128:(c + 1) * 128],
                        xn2v[:, c, tg * 512:(tg + 1) * 512],
                        start=(c == 0), stop=(c == NC - 1)),
                      reads=[res(f"wk{buf}"), res("big0")],
                      writes=[res(rn)] if c == 0 else (),
                      inc=("t", 1) if c == NC - 1 else None)
                o("scalar", lambda e, pp=pp, tg=tg: e.activation(
                    k0[:, tg * 512:(tg + 1) * 512], pp, AF.Copy),
                  reads=[res(rn)], writes=[res("k0")], inc=("sc", 1))
            for tg in range(4):
                pp, rn = (pAB[:, 0:512], "pAB0") if tg % 2 == 0 else \
                         (pAB[:, 512:1024], "pAB1")
                for c in range(NC):
                    o("tensor", lambda e, pp=pp, c=c, tg=tg, buf=buf: e.matmul(
                        pp, wvs[:, c * 128:(c + 1) * 128],
                        xn2v[:, c, tg * 512:(tg + 1) * 512],
                        start=(c == 0), stop=(c == NC - 1)),
                      reads=[res("wv"), res("big0")],
                      writes=[res(rn)] if c == 0 else (),
                      inc=("t", 1) if c == NC - 1 else None)
                o("scalar", lambda e, pp=pp, tg=tg: e.activation(
                    vfm[:, tg * 512:(tg + 1) * 512], pp, AF.Copy),
                  reads=[res(rn)], writes=[res("vfm")], inc=("sc", 1))
            for g in range(2):
                pt, names = (pF, ("pF0", "pF1")) if g % 2 == 0 else \
                            (pE, ("pE0", "pE1"))
                rs = [res(names[0]), res(names[1])]
                for c8 in range(8):
                    tc = g * 8 + c8
                    o("tensor", lambda e, tc=tc, c8=c8, pt=pt: e.matmul(
                        pt[:, c8 * 128:(c8 + 1) * 128],
                        vfm[:, tc * 128:(tc + 1) * 128], idb[:, :],
                        start=True, stop=True),
                      reads=[res("vfm"), res("idb")],
                      writes=rs if c8 == 0 else (),
                      inc=("t", 1) if c8 == 7 else None)
                o("scalar", lambda e, g=g, pt=pt: e.activation(
                    v0[:, g * 1024:(g + 1) * 1024], pt[:, 0:1024], AF.Copy),
                  reads=rs, writes=[res("v0")], inc=("sc", 1))
            for tg in range(2):
                pp, rn = (pAB[:, 0:512], "pAB0") if tg % 2 == 0 else \
                         (pAB[:, 512:1024], "pAB1")
                for c in range(NC):
                    o("tensor", lambda e, pp=pp, c=c, tg=tg, buf=buf: e.matmul(
                        pp, wqs[buf][:, c * 128:(c + 1) * 128],
                        xn2v[:, c, 1024 + tg * 512: 1024 + (tg + 1) * 512],
                        start=(c == 0), stop=(c == NC - 1)),
                      reads=[res(f"wq{buf}"), res("big0")],
                      writes=[res(rn)] if c == 0 else (),
                      inc=("t", 1) if c == NC - 1 else None)
                o("scalar", lambda e, pp=pp, tg=tg: e.activation(
                    q0[:, tg * 512:(tg + 1) * 512], pp, AF.Copy),
                  reads=[res(rn)], writes=[res("q0")], inc=("sc", 1))
            if hh + 1 < NH:
                nb = (hh + 1) % 2
                ld("gpsimd", "sg", wqs[nb][:, :], wq_d[hh + 1], res(f"wq{nb}"))
                ld("gpsimd", "sg", wks[nb][:, :], wk_d[hh + 1], res(f"wk{nb}"))

            def rope(tb, rtb, ntok, csoff):
                nseg = ntok // 512
                for sg_i in range(nseg):
                    pp, rn = (pE[:, 0:512], "pE0") if sg_i % 2 == 0 else \
                             (pE[:, 512:1024], "pE1")
                    sl = slice(sg_i * 512, (sg_i + 1) * 512)
                    co = csoff + sg_i * 512
                    o("tensor", lambda e, pp=pp, tb=tb, sl=sl: e.matmul(
                        pp, psw[:, :], tb[:, sl], start=True, stop=True),
                      reads=[res("psw"), rtb], writes=[res(rn)], inc=("t", 1))
                    o("vector", lambda e, pp=pp, sl=sl, co=co: e.tensor_tensor(
                        rtmp[:, sl], pp, cs_s[:, 2048 + co: 2048 + co + 512],
                        ALU.mult),
                      reads=[res(rn), res("cs")], writes=[res("rtmp")])
                    o("vector", lambda e, tb=tb, sl=sl, co=co: e.tensor_tensor(
                        tb[:, sl], tb[:, sl], cs_s[:, co:co + 512], ALU.mult),
                      reads=[res("cs")], writes=[rtb])
                    o("vector", lambda e, tb=tb, sl=sl: e.tensor_tensor(
                        tb[:, sl], tb[:, sl], rtmp[:, sl], ALU.add),
                      reads=[res("rtmp")], writes=[rtb], inc=("v", 1))

            rope(k0, res("k0"), 2048, 0)
            rope(q0, res("q0"), 1024, 1024)
            o("sync", lambda e, hh=hh: e.dma_start(out=k_d[hh], in_=k0[:, :]),
              reads=[res("k0")], writes=[res("k_d")], inc=("sx", 16))
            o("sync", lambda e, hh=hh: e.dma_start(out=q_d[hh], in_=q0[:, :]),
              reads=[res("q0")], writes=[res("q_d")], inc=("sx", 16))
            o("sync", lambda e, hh=hh: e.dma_start(out=v_d[hh], in_=v0[:, :]),
              reads=[res("v0")], writes=[res("v_d")], inc=("sx", 16))

        B.phase = "P3"
        B.barrier()
        def pmn(j):
            return max(0, (j - 8) // 2)

        def pmx(j):
            return min(3, j // 2)

        poffs = []
        acc_off = 0
        for j in range(NT):
            poffs.append(acc_off)
            acc_off += 256 * (pmx(j) - pmn(j) + 1)
        assert acc_off == 10240, acc_off

        def emit_A_j(hh, j):
            buf = hh % 2
            rk, rq = res(f"ks{buf}"), res(f"qs{buf}")
            w = 256 * (pmx(j) - pmn(j) + 1)
            qlo = pmn(j) * 256
            pp, rn0, rn1 = (pAB, "pAB0", "pAB1") if j % 2 == 0 else \
                           (pCD, "pCD0", "pCD1")
            w1 = min(w, 512)
            o("tensor", lambda e, pp=pp, j=j, w1=w1, qlo=qlo, buf=buf:
              e.matmul(pp[:, 0:w1], ks[buf][:, j * 128:(j + 1) * 128],
                       qs[buf][:, qlo:qlo + w1], start=True, stop=True),
              reads=[rk, rq], writes=[res(rn0)],
              inc=("t", 1) if w <= 512 else None)
            if w > 512:
                o("tensor", lambda e, pp=pp, j=j, w=w, qlo=qlo, buf=buf:
                  e.matmul(pp[:, 512:w], ks[buf][:, j * 128:(j + 1) * 128],
                           qs[buf][:, qlo + 512:qlo + w],
                           start=True, stop=True),
                  reads=[rk, rq], writes=[res(rn1)], inc=("t", 1))
            rps = [res(rn0)] + ([res(rn1)] if w > 512 else [])
            po = poffs[j]
            o("scalar", lambda e, pp=pp, w=w, j=j, po=po, buf=buf:
              e.activation(
                pb[buf][:, po:po + w], pp[:, 0:w], AF.Exp,
                bias=exbs[:, j:j + 1], scale=SCALE),
              reads=rps + [res("exb")], writes=[res(f"pb{buf}")],
              inc=("sc", 1))
            for p in range(pmn(j), pmx(j) + 1):
                for sl_i, s in enumerate((2 * p, 2 * p + 1)):
                    ci = j - s
                    off = po + (p - pmn(j)) * 256 + sl_i * 128
                    if ci == 0:
                        mt, mres = atrb, "atri"
                    elif ci == 8:
                        mt, mres = trib, "tri"
                    elif 0 < ci < 8:
                        continue
                    else:
                        o("vector", lambda e, off=off, buf=buf: e.memset(
                            pb[buf][:, off:off + 128], 0.0),
                          writes=[res(f"pb{buf}")], inc=None)
                        continue
                    o("vector", lambda e, off=off, mt=mt, buf=buf:
                      e.tensor_tensor(
                        pb[buf][:, off:off + 128], pb[buf][:, off:off + 128],
                        mt[:, :], ALU.mult),
                      reads=[res(mres)], writes=[res(f"pb{buf}")], inc=None)
            o("vector", lambda e: e.nop(), writes=[res(f"pb{buf}")],
              inc=("v", 1))

        def emit_B_pair(hh, p):
            buf = hh % 2
            vb = hh % 3
            rv = res(f"vs{vb}")
            for idx, j in enumerate(range(2 * p, 2 * p + 10)):
                off = poffs[j] + (p - pmn(j)) * 256
                o("tensor", lambda e, p=p, j=j, off=off, idx=idx, buf=buf,
                  vb=vb: e.matmul(
                    pE[:, p * 256:(p + 1) * 256],
                    vs[vb][:, j * 128:(j + 1) * 128],
                    pb[buf][:, off:off + 256],
                    start=(idx == 0), stop=(idx == 9)),
                  reads=[res(f"pb{buf}"), rv],
                  writes=[res("pE0") if p < 2 else res("pE1")]
                  if idx == 0 else (),
                  inc=None)
                o("tensor", lambda e, p=p, j=j, off=off, idx=idx, buf=buf:
                  e.matmul(
                    pF[0:1, p * 256:(p + 1) * 256], oneb[:, 0:1],
                    pb[buf][:, off:off + 256],
                    start=(idx == 0), stop=(idx == 9)),
                  reads=[res(f"pb{buf}"), res("oneb")],
                  writes=[res("pF0") if p < 2 else res("pF1")]
                  if idx == 0 else (),
                  inc=("t", 1) if idx == 9 else None)

        def emit_tail(hh):
            buf = hh % 2
            if debug:
                o("sync", lambda e, hh=hh: e.dma_start(
                    out=dpb_d, in_=pb[0][:, :]),
                  reads=[res("pb0")], writes=[res("dpb")], inc=("sx", 16)) \
                  if hh == 0 else None
            o("vector", lambda e: e.tensor_copy(denr[0:1, :], pF[0:1, 0:1024]),
              reads=[res("pF0"), res("pF1")], writes=[res("denr")], inc=("v", 1))
            if debug:
                o("sync", lambda e, hh=hh: e.dma_start(
                    out=dden_d[hh], in_=denr[0:1, :]),
                  reads=[res("denr")], writes=[res("dden")], inc=("sx", 16))
            o("vector", lambda e: e.reciprocal(recr[0:1, :], denr[0:1, :]),
              reads=[res("denr")], writes=[res("recr")], inc=("v", 1))
            o("vector", lambda e: e.tensor_copy(rec16[0:1, :], recr[0:1, :]),
              reads=[res("recr")], writes=[res("rec16")], inc=("v", 1))
            o("tensor", lambda e: e.matmul(
                pF[:, 0:512], oneb[0:1, :], rec16[0:1, 0:512],
                start=True, stop=True),
              reads=[res("rec16"), res("oneb")], writes=[res("pF0")], inc=None)
            o("tensor", lambda e: e.matmul(
                pF[:, 512:1024], oneb[0:1, :], rec16[0:1, 512:1024],
                start=True, stop=True),
              reads=[res("rec16"), res("oneb")], writes=[res("pF1")],
              inc=("t", 1))
            o("vector", lambda e: e.tensor_copy(recb[:, :], pF[:, 0:1024]),
              reads=[res("pF0"), res("pF1")], writes=[res("recb")],
              inc=("v", 1))
            if debug:
                o("sync", lambda e, hh=hh: e.dma_start(
                    out=drec_d[hh], in_=recb[:, :]),
                  reads=[res("recb")], writes=[res("drec")], inc=("sx", 16))
            o("vector", lambda e, buf=buf: e.tensor_tensor(
                atts[buf][:, :], pE[:, 0:1024], recb[:, :], ALU.mult),
              reads=[res("pE0"), res("pE1"), res("recb")],
              writes=[res(f"atts{buf}")], inc=("v", 1))
            o("sync", lambda e, hh=hh, buf=buf: e.dma_start(
                out=att_d[hh], in_=atts[buf][:, :]),
              reads=[res(f"atts{buf}")], writes=[res("att_d")], inc=("sx", 16))

        ld("sync", "sx", ks[0][:, :], k_d[0], res("ks0"))
        ld("sync", "sx", vs[0][:, :], v_d[0], res("vs0"))
        ld("sync", "sx", qs[0][:, :], q_d[0], res("qs0"))
        for hh in range(NH):
            if hh + 1 < NH:
                nb = (hh + 1) % 2
                vb = (hh + 1) % 3
                ld("sync", "sx", ks[nb][:, :], k_d[hh + 1], res(f"ks{nb}"))
                ld("sync", "sx", vs[vb][:, :], v_d[hh + 1], res(f"vs{vb}"))
                ld("sync", "sx", qs[nb][:, :], q_d[hh + 1], res(f"qs{nb}"))
            for j in range(NT):
                emit_A_j(hh, j)
                if hh > 0 and j % 4 == 3:
                    emit_B_pair(hh - 1, j // 4)
            if hh > 0:
                emit_tail(hh - 1)
        for p in range(4):
            emit_B_pair(NH - 1, p)
        emit_tail(NH - 1)

        B.phase = "P4"
        B.barrier()
        o("sync", lambda e: e.dma_start(
            out=attV[:, :, :], in_=att_d.rearrange("h p q -> p h q")),
          reads=[res("att_d")], writes=[res("attR")], inc=("sx", 16))
        for os_ in range(8):
            wb = os_ % 2
            o("gpsimd", lambda e, os_=os_, wb=wb: e.dma_start(
                out=woH[wb], in_=wo_d[:, :, os_ * 512:(os_ + 1) * 512]),
              writes=[res(f"wo{wb}")], inc=("sg", 16))
            for qt in range(QT):
                pn = ["pAB0", "pAB1", "pCD0", "pCD1"][qt % 4]
                pp = {"pAB0": pAB[:, 0:512], "pAB1": pAB[:, 512:1024],
                      "pCD0": pCD[:, 0:512], "pCD1": pCD[:, 512:1024]}[pn]
                for c in range(NC):
                    o("tensor", lambda e, pp=pp, c=c, qt=qt, wb=wb: e.matmul(
                        pp, attV[:, c, qt * 128:(qt + 1) * 128],
                        woH[wb][:, c, :], start=(c == 0), stop=(c == NC - 1)),
                      reads=[res("attR"), res(f"wo{wb}")],
                      writes=[res(pn)] if c == 0 else (),
                      inc=("t", 1) if c == NC - 1 else None)
                mb = qt % 2
                o("sync", lambda e, os_=os_, qt=qt, mb=mb: e.dma_start(
                    out=xms[mb],
                    in_=xmid_d[1024 + qt * 128:1024 + (qt + 1) * 128,
                               os_ * 512:(os_ + 1) * 512]),
                  reads=[res("xmid_d")], writes=[res(f"xms{mb}")],
                  inc=("sx", 16))
                o("vector", lambda e, pp=pp, mb=mb: e.tensor_tensor(
                    xps[mb], xms[mb], pp, ALU.add),
                  reads=[res(pn), res(f"xms{mb}")], writes=[res(f"xps{mb}")],
                  inc=("v", 1))
                o("sync", lambda e, os_=os_, qt=qt, mb=mb: e.dma_start(
                    out=xpost_d[qt * 128:(qt + 1) * 128,
                                os_ * 512:(os_ + 1) * 512], in_=xps[mb]),
                  reads=[res(f"xps{mb}")], writes=[res("xpost_d")],
                  inc=("sx", 16))

        B.phase = "P5"
        B.barrier()
        ld("gpsimd", "sg", rAs[:, :], ra_d.rearrange("p c e -> p (c e)"),
           res("rAs"))
        ld("gpsimd", "sg", bcs[0:64, :], bc_d, res("bcs"))
        for qt in range(QT):
            xr = xrs[qt % 2]
            rxr = res(f"xr{qt % 2}")
            o("sync", lambda e, qt=qt, xr=xr: e.dma_start(
                out=xr, in_=xpost_d[qt * 128:(qt + 1) * 128, :]),
              reads=[res("xpost_d")], writes=[rxr], inc=("sx", 16))
            ln_stats(xr, rxr)
            o("vector", lambda e, xr=xr: e.tensor_scalar(
                xntm, xr, mcol, rstd, ALU.subtract, ALU.mult),
              reads=[rxr], sreads=[res("stt")], writes=[res("xntm")],
              inc=("v", 1))
            transpose_set(xntm, res("xntm"),
                          lambda e, g, src: e.tensor_copy(
                              xnfm[:, g * 1024:(g + 1) * 1024], src),
                          res("xnfm"))
            ffn_tile(xnfm, res("xnfm"), xr, rxr)
            o("sync", lambda e, qt=qt, xr=xr: e.dma_start(
                out=out_d[qt * 128:(qt + 1) * 128, :], in_=xr),
              reads=[rxr], writes=[res("out_d")], inc=("sx", 16))

        fin = B.val("sx")
        B.ops.append(("sync", lambda e: e.nop(), [("sx", fin)], []))
        fin_g = B.val("sg")
        B.ops.append(("gpsimd", lambda e: e.nop(), [("sg", fin_g)], []))
        fin_b = B.val("sb")
        B.ops.append(("scalar", lambda e: e.nop(), [("sb", fin_b)], []))

        def mk(eng_name):
            def body(e):
                for en, fn, waits, incs in B.ops:
                    if en != eng_name:
                        continue
                    for s, vv in waits:
                        e.wait_ge(sems[s], vv)
                    instr = fn(e)
                    for s, amt in incs:
                        instr.then_inc(sems[s], amt)
            body.__name__ = f"_{eng_name}"
            return body

        block.tensor(mk("tensor"))
        block.vector(mk("vector"))
        block.scalar(mk("scalar"))
        block.sync(mk("sync"))
        block.gpsimd(mk("gpsimd"))

    return nc



def _prep_core_inputs(x_ctx, h, Wq, Wk, Wv, Wo, rw, A, Bm):
    D_ = 4096
    bf = ml_dtypes.bfloat16

    def blk_qkv(W):
        return np.ascontiguousarray(
            W.T.reshape(32, 128, 32, 128).transpose(2, 1, 0, 3)
            .reshape(32, 128, 4096))

    wo_blk = np.ascontiguousarray(
        Wo.T.reshape(32, 128, D_).transpose(1, 0, 2))
    acat = A.reshape(64, D_)
    rA = np.concatenate([acat.T, rw.T], axis=1)
    ra_blk = np.ascontiguousarray(rA.reshape(32, 128, 68).transpose(1, 0, 2))
    bcat = np.ascontiguousarray(Bm.transpose(0, 2, 1).reshape(64, D_))

    pos = (np.arange(2048) - 1024 * (1 - h)).astype(np.float64)
    inv = 1.0 / 10000.0 ** (np.arange(0, 128, 2, dtype=np.float64) / 128)
    fr = pos[None, :] * inv[:, None]
    cos = np.concatenate([np.cos(fr), np.cos(fr)], axis=0)
    sin = np.concatenate([-np.sin(fr), np.sin(fr)], axis=0)
    cs = np.concatenate([cos, sin], axis=1).astype(np.float32)

    kk = np.arange(128)[:, None]
    rr = np.arange(128)[None, :]
    tri = (kk <= rr).astype(bf)
    atri = (kk > rr).astype(bf)
    expb = np.full((128, 16), SHIFT, np.float32)
    if h == 0:
        expb[:, :8] = NEGB

    return {
        "x": np.ascontiguousarray(x_ctx, dtype=np.float32),
        "wq": blk_qkv(Wq), "wk": blk_qkv(Wk), "wv": blk_qkv(Wv),
        "wo": wo_blk,
        "ra": ra_blk.astype(np.float32),
        "bc": bcat.astype(np.float32),
        "cs": cs,
        "identb": np.eye(128, dtype=bf),
        "identf": np.eye(128, dtype=np.float32),
        "pswap": np.roll(np.eye(128), 64, axis=0).astype(bf),
        "onesb": np.ones((128, 128), dtype=bf),
        "tri": tri, "atri": atri, "expb": expb,
    }


def kernel(L0_attn_x, L0_attn_Wq, L0_attn_Wk, L0_attn_Wv, L0_attn_Wo,
           L0_ffn_router_w, L0_ffn_A, L0_ffn_B):
    x = np.asarray(L0_attn_x, dtype=np.float32)
    Wq = np.asarray(L0_attn_Wq, np.float32)
    Wk = np.asarray(L0_attn_Wk, np.float32)
    Wv = np.asarray(L0_attn_Wv, np.float32)
    Wo = np.asarray(L0_attn_Wo, np.float32)
    rw = np.asarray(L0_ffn_router_w, np.float32)
    A = np.asarray(L0_ffn_A, np.float32)
    Bm = np.asarray(L0_ffn_B, np.float32)

    nc = build_program(h=0)
    in_maps = []
    for c in range(8):
        b, h = c // 2, c % 2
        if h == 0:
            x_ctx = np.concatenate(
                [np.zeros((1024, 4096), np.float32), x[b, :1024]], axis=0)
        else:
            x_ctx = x[b]
        in_maps.append(_prep_core_inputs(x_ctx, h, Wq, Wk, Wv, Wo, rw, A, Bm))

    result = run_bass_kernel_spmd(nc, in_maps, core_ids=list(range(8)))
    out = np.empty((4, 2048, 4096), np.float32)
    for c in range(8):
        b, h = c // 2, c % 2
        out[b, h * 1024:(h + 1) * 1024] = result.results[c]["out"]
    return out


# version 10
# speedup vs baseline: 1.2580x; 1.2580x over previous
import math
import contextlib
import numpy as np
import ml_dtypes
import concourse.bass as bass
import concourse.mybir as mybir
from concourse.bass_utils import run_bass_kernel_spmd

F32 = mybir.dt.float32
BF16 = mybir.dt.bfloat16
U8 = mybir.dt.uint8
AF = mybir.ActivationFunctionType
ALU = mybir.AluOpType
AX = mybir.AxisListType

D = 4096
CTX = 2048
NQ = 1024
NH = 32
HD = 128
NT = CTX // 128
QT = NQ // 128
NC = 32
SCALE = 1.0 / math.sqrt(128.0)
SHIFT = -12.0
NEGB = -30000.0
EPS = 1e-5

ENGINES = ("tensor", "vector", "scalar", "sync", "gpsimd")
ESEM = {"tensor": "t", "vector": "v", "scalar": "sc"}


class Res:
    __slots__ = ("name", "w", "r")

    def __init__(self, name):
        self.name = name
        self.w = None
        self.r = {}


SERIAL = False
SERIAL_PHASES = {"G"}


class Builder:
    def __init__(self):
        self.phase = "P0"
        self.ops = []
        self.cnt = {}
        self.waited = {e: {} for e in ENGINES}
        self.pend = {e: {"r": [], "w": []} for e in ENGINES}
        self.bar = []

    def val(self, sem):
        return self.cnt.get(sem, 0)

    def _bump(self, sem, amt):
        self.cnt[sem] = self.cnt.get(sem, 0) + amt
        return self.cnt[sem]

    def barrier(self):
        self.bar = [(s, v) for s, v in self.cnt.items()]

    def op(self, eng, fn, reads=(), writes=(), inc=None, sreads=()):
        if SERIAL or self.phase in SERIAL_PHASES:
            if not SERIAL and self.phase == "G":
                self.bar = [(s, v) for s, v in self.cnt.items()
                            if s in ("t", "v", "sc")]
            else:
                self.bar = [(s, v) for s, v in self.cnt.items()]
            if inc is None:
                inc = (ESEM.get(eng, "sx"), 1) if eng in ESEM else None
            sreads = ()
            waits0 = list(self.bar)
            for r0 in list(reads) + list(writes):
                if r0.w:
                    waits0.append(r0.w)
            best0 = {}
            for s, v in waits0:
                if v > best0.get(s, -1):
                    best0[s] = v
            wd0 = self.waited[eng]
            out_w = []
            for s, v in best0.items():
                if wd0.get(s, -1) >= v:
                    continue
                wd0[s] = v
                out_w.append((s, v))
            incs = []
            if inc is not None:
                sem, amt = inc
                val = self._bump(sem, amt)
                incs.append((sem, amt))
                for r0 in list(reads):
                    if r0.r.get(sem, -1) < val:
                        r0.r[sem] = val
                for r0 in list(writes):
                    r0.w = (sem, val)
                    r0.r = {}
            self.ops.append((eng, fn, out_w, incs))
            return
        waits = list(self.bar)
        for res in reads:
            if res.w:
                waits.append(res.w)
        hard_sems = set()
        for res in sreads:
            if res.w:
                waits.append(res.w)
                hard_sems.add(res.w[0])
        for res in writes:
            if res.w:
                waits.append(res.w)
            waits.extend(res.r.items())
        own = ESEM.get(eng)
        wd = self.waited[eng]
        best = {}
        for s, v in waits:
            if v > best.get(s, -1):
                best[s] = v
        out_w = []
        for s, v in best.items():
            if s == own and s not in hard_sems:
                continue
            if wd.get(s, -1) >= v:
                continue
            wd[s] = v
            out_w.append((s, v))
        incs = []
        if inc is not None:
            sem, amt = inc
            val = self._bump(sem, amt)
            incs.append((sem, amt))
            for res in list(reads) + list(sreads) + self.pend[eng]["r"]:
                if res.r.get(sem, -1) < val:
                    res.r[sem] = val
            for res in list(writes) + self.pend[eng]["w"]:
                res.w = (sem, val)
                res.r = {}
            self.pend[eng] = {"r": [], "w": []}
        else:
            self.pend[eng]["r"].extend(reads)
            self.pend[eng]["r"].extend(sreads)
            self.pend[eng]["w"].extend(writes)
        self.ops.append((eng, fn, out_w, incs))


def build_program(h, debug=False):
    nc = bass.Bass("TRN2", target_bir_lowering=False, debug=False)

    dt = nc.dram_tensor
    x_d = dt("x", [CTX, D], F32, kind="ExternalInput").ap()
    wq_d = dt("wq", [NH, 128, D], F32, kind="ExternalInput").ap()
    wk_d = dt("wk", [NH, 128, D], F32, kind="ExternalInput").ap()
    wv_d = dt("wv", [NH, 128, D], F32, kind="ExternalInput").ap()
    wo_d = dt("wo", [128, NC, D], F32, kind="ExternalInput").ap()
    ra_d = dt("ra", [128, NC, 68], F32, kind="ExternalInput").ap()
    bc_d = dt("bc", [64, D], F32, kind="ExternalInput").ap()
    cs_d = dt("cs", [128, 2 * CTX], F32, kind="ExternalInput").ap()
    idb_d = dt("identb", [128, 128], BF16, kind="ExternalInput").ap()
    idf_d = dt("identf", [128, 128], F32, kind="ExternalInput").ap()
    psw_d = dt("pswap", [128, 128], BF16, kind="ExternalInput").ap()
    one_d = dt("onesb", [128, 128], BF16, kind="ExternalInput").ap()
    tri_d = dt("tri", [128, 128], BF16, kind="ExternalInput").ap()
    atr_d = dt("atri", [128, 128], BF16, kind="ExternalInput").ap()
    exb_d = dt("expb", [128, NT], F32, kind="ExternalInput").ap()

    ik = "ExternalOutput" if debug else "Internal"
    xmid_d = dt("xmid", [CTX, D], F32, kind=ik).ap()
    q_d = dt("qh", [NH, 128, NQ], BF16, kind=ik).ap()
    k_d = dt("kh", [NH, 128, CTX], BF16, kind=ik).ap()
    v_d = dt("vh", [NH, 128, CTX], BF16, kind=ik).ap()
    att_d = dt("atth", [NH, 128, NQ], BF16, kind=ik).ap()
    xpost_d = dt("xpost", [NQ, D], F32, kind=ik).ap()
    rec_d = dt("recb", [1, NQ], F32, kind="Internal").ap()
    out_d = dt("out", [NQ, D], F32, kind="ExternalOutput").ap()
    if debug:
        dstt_d = dt("dstt", [NT, 128, 96], F32, kind="ExternalOutput").ap()
        dxn1_d = dt("dxn1", [NT, 128, D], BF16, kind="ExternalOutput").ap()
        dgat_d = dt("dgat", [NT, 128, 4], BF16, kind="ExternalOutput").ap()
        dxnt_d = dt("dxnt", [NT, 128, D], BF16, kind="ExternalOutput").ap()
        dpb_d = dt("dpb", [128, 10240], BF16, kind="ExternalOutput").ap()
        dden_d = dt("dden", [NH, 1, 1024], F32, kind="ExternalOutput").ap()
        drec_d = dt("drec", [NH, 128, 1024], F32, kind="ExternalOutput").ap()
        dlgb_d = dt("dlgb", [NT, 4, 128], F32, kind="ExternalOutput").ap()
        dgsc_d = dt("dgsc", [NT, 128, 16], F32, kind="ExternalOutput").ap()

    ABYTES = 63488

    with contextlib.ExitStack() as ctx:
        E = ctx.enter_context
        big0 = E(nc.sbuf_tensor("big0", [128, 131072], U8))
        aren = E(nc.sbuf_tensor("aren", [128, ABYTES], U8))
        cs_s = E(nc.sbuf_tensor("cs_s", [128, 2 * CTX], F32))
        idb = E(nc.sbuf_tensor("idb", [128, 128], BF16))
        idf = E(nc.sbuf_tensor("idf", [128, 128], F32))
        psw = E(nc.sbuf_tensor("psw", [128, 128], BF16))
        oneb = E(nc.sbuf_tensor("oneb", [128, 128], BF16))
        trib = E(nc.sbuf_tensor("trib", [128, 128], BF16))
        atrb = E(nc.sbuf_tensor("atrb", [128, 128], BF16))
        exbs = E(nc.sbuf_tensor("exbs", [128, NT], F32))
        epsb = E(nc.sbuf_tensor("epsb", [128, 1], F32))
        pAB = E(nc.psum_tensor("pAB", [128, 1024], F32))
        pCD = E(nc.psum_tensor("pCD", [128, 1024], F32))
        pE = E(nc.psum_tensor("pE", [128, 1024], F32))
        pF = E(nc.psum_tensor("pF", [128, 1024], F32))
        sems = {n: E(nc.semaphore(n)) for n in ("t", "v", "sc", "sx", "sg", "sb")}
        block = E(nc.Block())

        B = Builder()
        R = {}

        def res(name):
            if name not in R:
                R[name] = Res(name)
            return R[name]

        def av(off, n, dtype):
            sz = 2 if dtype == BF16 else 4
            return aren[:, off:off + n * sz].bitcast(dtype)

        big_bf = big0[:, :].bitcast(BF16)
        xn2v = big_bf.rearrange("p (c t) -> p c t", c=NC)

        xr0 = av(0, 4096, F32)
        xr1 = av(16384, 4096, F32)
        xntm = av(32768, 4096, BF16)
        xnfm = av(40960, 4096, BF16)
        rAs = av(49152, 2176, BF16)
        bcs = av(53504, 4096, BF16)
        smidb = av(61696, 128, BF16)
        lgb = av(61952, 128, F32)
        gsc = av(62464, 32, F32)
        gbf = av(62592, 4, BF16)
        g64t = av(62720, 64, BF16)
        g64s = av(62848, 128, BF16)
        stt = av(63104, 96, F32)
        wqs = [av(0, 4096, BF16), av(8192, 4096, BF16)]
        wks = [av(16384, 4096, BF16), av(24576, 4096, BF16)]
        wvs = av(32768, 4096, BF16)
        q0 = av(40960, 1024, BF16)
        k0 = av(43008, 2048, BF16)
        v0 = av(47104, 2048, BF16)
        vfm = av(51200, 2048, BF16)
        rtmp = av(55296, 2048, BF16)
        ks = [av(0, 2048, BF16), av(4096, 2048, BF16)]
        vs = [av(8192, 2048, BF16), av(12288, 2048, BF16),
              av(36864, 2048, BF16)]
        qs = [av(16384, 1024, BF16), av(18432, 1024, BF16)]
        rec16 = av(40960, 1024, BF16)
        pb = [big_bf[:, 0:10240], big_bf[:, 10240:20480]]
        atts = [av(20480, 1024, BF16), av(22528, 1024, BF16)]
        recb = av(24576, 1024, F32)
        denr = av(28672, 1024, F32)
        recr = av(32768, 1024, F32)
        xms = [av(0, 512, F32), av(2048, 512, F32)]
        xps = [av(4096, 512, F32), av(6144, 512, F32)]
        woH = [big_bf[:, 0:16384].rearrange("p (c o) -> p c o", c=NC),
               big_bf[:, 16384:32768].rearrange("p (c o) -> p c o", c=NC)]
        attV = big_bf[:, 32768:65536].rearrange("p (c q) -> p c q", c=NC)

        o = B.op

        def ld(eng, sem, out, in_, wres):
            o(eng, lambda e, a=out, b=in_: e.dma_start(out=a, in_=b),
              writes=[wres], inc=(sem, 16))

        ld("sync", "sx", cs_s[:, :], cs_d, res("cs"))
        ld("sync", "sx", idb[:, :], idb_d, res("idb"))
        ld("sync", "sx", idf[:, :], idf_d, res("idf"))
        ld("sync", "sx", psw[:, :], psw_d, res("psw"))
        ld("sync", "sx", oneb[:, :], one_d, res("oneb"))
        ld("sync", "sx", trib[:, :], tri_d, res("tri"))
        ld("sync", "sx", atrb[:, :], atr_d, res("atri"))
        ld("sync", "sx", exbs[:, :], exb_d, res("exb"))
        ld("gpsimd", "sg", rAs[:, :], ra_d.rearrange("p c e -> p (c e)"), res("rAs"))
        ld("gpsimd", "sg", bcs[0:64, :], bc_d, res("bcs"))
        o("vector", lambda e: e.memset(epsb[:, 0:1], EPS), writes=[res("eps")],
          inc=("v", 1))

        bst = stt[:, 0:48]
        magg = stt[:, 48:50]
        mcol = stt[:, 48:49]
        vvar = stt[:, 49:50]
        rstd = stt[:, 51:52]; lnv = stt[:, 52:53]

        def ln_stats(xr, rx, junk=None, rjunk=None):
            for u in range(8):
                o("vector", lambda e, u=u: e.bn_stats(
                    bst[:, u * 6:(u + 1) * 6], xr[:, u * 512:(u + 1) * 512]),
                  reads=[rx], writes=[res("stt")])
            o("vector", lambda e: e.bn_aggr(magg, bst),
              writes=[res("stt")], inc=("v", 1))
            o("scalar", lambda e: e.activation(lnv, vvar, AF.Ln, bias=epsb[:, 0:1]),
              reads=[res("stt"), res("eps")], writes=[res("stt")])
            o("scalar", lambda e: e.activation(rstd, lnv, AF.Exp, scale=-0.5),
              writes=[res("stt")], inc=("sc", 1))

        def gates_and_smid(rmid, dbg_i=None):
            gls = gsc[:, 0:4]; gi = gsc[:, 4:8]; gm = gsc[:, 8:12]
            gm1 = gsc[:, 12:13]; gm2 = gsc[:, 13:14]
            gden = gsc[:, 14:15]; grec = gsc[:, 15:16]
            glg = gsc[:, 16:20]
            o("vector", lambda e: e.tensor_copy(glg, pCD[:, 0:4]),
              reads=[res("pCD0")], writes=[res("gsc")], inc=("v", 1))
            o("vector", lambda e: e.reduce_max(gm1, glg, axis=AX.X),
              writes=[res("gsc")], inc=("v", 1))
            o("vector", lambda e: e.tensor_scalar(gls, glg, gm1, None,
                                                  ALU.subtract),
              sreads=[res("gsc")], writes=[res("gsc")], inc=("v", 1))
            o("vector", lambda e: e.tensor_scalar(gi, gls, 0.0, None, ALU.is_equal),
              writes=[res("gsc")])
            o("vector", lambda e: e.scalar_tensor_tensor(gm, gi, NEGB, gls,
                                                         ALU.mult, ALU.add),
              writes=[res("gsc")])
            o("vector", lambda e: e.reduce_max(gm2, gm, axis=AX.X),
              writes=[res("gsc")], inc=("v", 1))
            o("vector", lambda e: e.tensor_scalar(gi, gls, gm2, None, ALU.is_ge),
              sreads=[res("gsc")], writes=[res("gsc")], inc=("v", 1))
            o("scalar", lambda e: e.activation(gm, gls, AF.Exp),
              reads=[res("gsc")], writes=[res("gsc")], inc=("sc", 1))
            o("vector", lambda e: e.tensor_tensor(gls, gm, gi, ALU.mult),
              writes=[res("gsc")])
            o("vector", lambda e: e.reduce_sum(gden, gls, axis=AX.X),
              writes=[res("gsc")])
            o("vector", lambda e: e.reciprocal(grec, gden),
              writes=[res("gsc")], inc=("v", 1))
            o("vector", lambda e: e.tensor_scalar(gbf, gls, grec, 2.0,
                                                  ALU.mult, ALU.mult),
              sreads=[res("gsc")], writes=[res("gbf")])
            if debug and dbg_i is not None:
                o("sync", lambda e, i=dbg_i: e.dma_start(
                    out=dgsc_d[i], in_=gsc[:, 0:16]),
                  reads=[res("gsc")], writes=[res("dgsc")], inc=("sx", 16))
            o("vector", lambda e: e.tensor_copy(
                g64t[:, :], gbf.unsqueeze(2).broadcast_to((128, 4, 16))),
              reads=[res("gbf")], writes=[res("g64t")], inc=("v", 1))
            o("tensor", lambda e: e.matmul(
                pF[0:64, 4:132], g64t[:, :], idb[:, :],
                start=True, stop=True),
              reads=[res("g64t"), res("idb")], writes=[res("pF0")], inc=("t", 1))
            o("vector", lambda e: e.tensor_copy(
                g64s[0:64, :], pF[0:64, 4:132]),
              reads=[res("pF0")], writes=[res("g64s")], inc=("v", 1))
            o("vector", lambda e: e.tensor_tensor(
                smidb[0:64, :], pAB[0:64, 0:128], g64s[0:64, :], ALU.mult),
              reads=rmid + [res("g64s")], writes=[res("smid")], inc=("v", 1))

        def ffn_tile(xn_fm, rxn, xr, rxr, deltas_into_xr=True, dbg_i=None):
            for c in range(NC):
                o("tensor", lambda e, c=c: e.matmul(
                    pAB[0:68, 0:128], rAs[:, c * 68:(c + 1) * 68],
                    xn_fm[:, c * 128:(c + 1) * 128],
                    start=(c == 0), stop=(c == NC - 1)),
                  reads=[res("rAs"), rxn],
                  writes=[res("pAB0")] if c == 0 else (),
                  inc=("t", 1) if c == NC - 1 else None)
            ph0 = B.phase
            B.phase = "G"
            for c in range(NC):
                o("tensor", lambda e, c=c: e.matmul(
                    pCD[:, 0:4], xn_fm[:, c * 128:(c + 1) * 128],
                    rAs[:, c * 68 + 64:c * 68 + 68],
                    start=(c == 0), stop=(c == NC - 1)),
                  reads=[res("rAs"), rxn],
                  writes=[res("pCD0")] if c == 0 else (),
                  inc=("t", 1) if c == NC - 1 else None)
            gates_and_smid([res("pAB0")], dbg_i=dbg_i)
            B.phase = ph0
            for ob in range(8):
                pp, rn = (pE[:, 0:512], "pE0") if ob % 2 == 0 else \
                         (pE[:, 512:1024], "pE1")
                o("tensor", lambda e, pp=pp, ob=ob: e.matmul(
                    pp, smidb[0:64, :], bcs[0:64, ob * 512:(ob + 1) * 512],
                    start=True, stop=True),
                  reads=[res("smid"), res("bcs")], writes=[res(rn)], inc=("t", 1))
                o("vector", lambda e, pp=pp, ob=ob: e.tensor_tensor(
                    xr[:, ob * 512:(ob + 1) * 512], xr[:, ob * 512:(ob + 1) * 512],
                    pp, ALU.add),
                  reads=[res(rn), rxr], writes=[rxr], inc=("v", 1))

        def transpose_set(src_bf, rsrc, dst_fn, rdst, alt_dst=None):
            for g in range(4):
                pt, names = (pF, ("pF0", "pF1")) if g % 2 == 0 else \
                            (pE, ("pE0", "pE1"))
                rs = [res(names[0]), res(names[1])]
                for c8 in range(8):
                    c = g * 8 + c8
                    o("tensor", lambda e, c=c, c8=c8, pt=pt: e.matmul(
                        pt[:, c8 * 128:(c8 + 1) * 128],
                        src_bf[:, c * 128:(c + 1) * 128], idb[:, :],
                        start=True, stop=True),
                      reads=[rsrc, res("idb")], writes=rs if c8 == 0 else (),
                      inc=("t", 1) if c8 == 7 else None)
                if alt_dst is not None and g % 2 == 1:
                    o("scalar", lambda e, g=g, pt=pt: e.activation(
                        alt_dst(g), pt[:, 0:1024], AF.Copy),
                      reads=rs, writes=[rdst], inc=("sc", 1))
                else:
                    o("vector", lambda e, g=g, pt=pt: dst_fn(e, g, pt[:, 0:1024]),
                      reads=rs, writes=[rdst], inc=("v", 1))

        B.phase = "P1"
        B.barrier()
        xrs = [xr0, xr1]
        for i in range(NT):
            xr = xrs[i % 2]
            rxr = res(f"xr{i % 2}")
            o("sync", lambda e, i=i, xr=xr: e.dma_start(
                out=xr, in_=x_d[i * 128:(i + 1) * 128, :]),
              writes=[rxr], inc=("sx", 16))
            ln_stats(xr, rxr)
            if debug:
                o("sync", lambda e, i=i: e.dma_start(
                    out=dstt_d[i], in_=stt[:, :]),
                  reads=[res("stt")], writes=[res("dstt")], inc=("sx", 16))
            o("vector", lambda e, xr=xr: e.tensor_scalar(
                xntm, xr, mcol, rstd, ALU.subtract, ALU.mult),
              reads=[rxr, res("stt")], writes=[res("xntm")], inc=("v", 1))
            transpose_set(xntm, res("xntm"),
                          lambda e, g, src: e.tensor_copy(
                              xnfm[:, g * 1024:(g + 1) * 1024], src),
                          res("xnfm"),
                          alt_dst=lambda g: xnfm[:, g * 1024:(g + 1) * 1024])
            if debug:
                o("sync", lambda e, i=i: e.dma_start(
                    out=dxn1_d[i], in_=xnfm[:, :]),
                  reads=[res("xnfm")], writes=[res("dxn1")], inc=("sx", 16))
                o("sync", lambda e, i=i: e.dma_start(
                    out=dxnt_d[i], in_=xntm),
                  reads=[res("xntm")], writes=[res("dxnt")], inc=("sx", 16))
            ffn_tile(xnfm, res("xnfm"), xr, rxr, dbg_i=i)
            if debug:
                o("sync", lambda e, i=i: e.dma_start(
                    out=dgat_d[i], in_=gbf),
                  reads=[res("gbf")], writes=[res("dgat")], inc=("sx", 16))
            o("sync", lambda e, i=i, xr=xr: e.dma_start(
                out=xmid_d[i * 128:(i + 1) * 128, :], in_=xr),
              reads=[rxr], writes=[res("xmid_d")], inc=("sx", 16))
            ln_stats(xr, rxr)
            o("vector", lambda e, xr=xr: e.tensor_scalar(
                xntm, xr, mcol, rstd, ALU.subtract, ALU.mult),
              reads=[rxr, res("stt")], writes=[res("xntm")], inc=("v", 1))
            transpose_set(xntm, res("xntm"),
                          lambda e, g, src, i=i: e.tensor_copy(
                              xn2v[:, g * 8:(g + 1) * 8, i * 128:(i + 1) * 128],
                              src.rearrange("p (c t) -> p c t", c=8)),
                          res("big0"))

        B.phase = "P2"
        B.barrier()
        ld("gpsimd", "sg", wqs[0][:, :], wq_d[0], res("wq0"))
        ld("gpsimd", "sg", wks[0][:, :], wk_d[0], res("wk0"))
        for hh in range(NH):
            buf = hh % 2
            ld("gpsimd", "sg", wvs[:, :], wv_d[hh], res("wv"))
            for tg in range(4):
                pp, rn = (pAB[:, 0:512], "pAB0") if tg % 2 == 0 else \
                         (pAB[:, 512:1024], "pAB1")
                for c in range(NC):
                    o("tensor", lambda e, pp=pp, c=c, tg=tg, buf=buf: e.matmul(
                        pp, wks[buf][:, c * 128:(c + 1) * 128],
                        xn2v[:, c, tg * 512:(tg + 1) * 512],
                        start=(c == 0), stop=(c == NC - 1)),
                      reads=[res(f"wk{buf}"), res("big0")],
                      writes=[res(rn)] if c == 0 else (),
                      inc=("t", 1) if c == NC - 1 else None)
                o("scalar", lambda e, pp=pp, tg=tg: e.activation(
                    k0[:, tg * 512:(tg + 1) * 512], pp, AF.Copy),
                  reads=[res(rn)], writes=[res("k0")], inc=("sc", 1))
            for tg in range(4):
                pp, rn = (pAB[:, 0:512], "pAB0") if tg % 2 == 0 else \
                         (pAB[:, 512:1024], "pAB1")
                for c in range(NC):
                    o("tensor", lambda e, pp=pp, c=c, tg=tg, buf=buf: e.matmul(
                        pp, wvs[:, c * 128:(c + 1) * 128],
                        xn2v[:, c, tg * 512:(tg + 1) * 512],
                        start=(c == 0), stop=(c == NC - 1)),
                      reads=[res("wv"), res("big0")],
                      writes=[res(rn)] if c == 0 else (),
                      inc=("t", 1) if c == NC - 1 else None)
                o("scalar", lambda e, pp=pp, tg=tg: e.activation(
                    vfm[:, tg * 512:(tg + 1) * 512], pp, AF.Copy),
                  reads=[res(rn)], writes=[res("vfm")], inc=("sc", 1))
            for g in range(2):
                pt, names = (pF, ("pF0", "pF1")) if g % 2 == 0 else \
                            (pE, ("pE0", "pE1"))
                rs = [res(names[0]), res(names[1])]
                for c8 in range(8):
                    tc = g * 8 + c8
                    o("tensor", lambda e, tc=tc, c8=c8, pt=pt: e.matmul(
                        pt[:, c8 * 128:(c8 + 1) * 128],
                        vfm[:, tc * 128:(tc + 1) * 128], idb[:, :],
                        start=True, stop=True),
                      reads=[res("vfm"), res("idb")],
                      writes=rs if c8 == 0 else (),
                      inc=("t", 1) if c8 == 7 else None)
                o("scalar", lambda e, g=g, pt=pt: e.activation(
                    v0[:, g * 1024:(g + 1) * 1024], pt[:, 0:1024], AF.Copy),
                  reads=rs, writes=[res("v0")], inc=("sc", 1))
            for tg in range(2):
                pp, rn = (pAB[:, 0:512], "pAB0") if tg % 2 == 0 else \
                         (pAB[:, 512:1024], "pAB1")
                for c in range(NC):
                    o("tensor", lambda e, pp=pp, c=c, tg=tg, buf=buf: e.matmul(
                        pp, wqs[buf][:, c * 128:(c + 1) * 128],
                        xn2v[:, c, 1024 + tg * 512: 1024 + (tg + 1) * 512],
                        start=(c == 0), stop=(c == NC - 1)),
                      reads=[res(f"wq{buf}"), res("big0")],
                      writes=[res(rn)] if c == 0 else (),
                      inc=("t", 1) if c == NC - 1 else None)
                o("scalar", lambda e, pp=pp, tg=tg: e.activation(
                    q0[:, tg * 512:(tg + 1) * 512], pp, AF.Copy),
                  reads=[res(rn)], writes=[res("q0")], inc=("sc", 1))
            if hh + 1 < NH:
                nb = (hh + 1) % 2
                ld("gpsimd", "sg", wqs[nb][:, :], wq_d[hh + 1], res(f"wq{nb}"))
                ld("gpsimd", "sg", wks[nb][:, :], wk_d[hh + 1], res(f"wk{nb}"))

            def rope(tb, rtb, ntok, csoff):
                nseg = ntok // 512
                for sg_i in range(nseg):
                    pp, rn = (pE[:, 0:512], "pE0") if sg_i % 2 == 0 else \
                             (pE[:, 512:1024], "pE1")
                    sl = slice(sg_i * 512, (sg_i + 1) * 512)
                    co = csoff + sg_i * 512
                    o("tensor", lambda e, pp=pp, tb=tb, sl=sl: e.matmul(
                        pp, psw[:, :], tb[:, sl], start=True, stop=True),
                      reads=[res("psw"), rtb], writes=[res(rn)], inc=("t", 1))
                    o("vector", lambda e, pp=pp, sl=sl, co=co: e.tensor_tensor(
                        rtmp[:, sl], pp, cs_s[:, 2048 + co: 2048 + co + 512],
                        ALU.mult),
                      reads=[res(rn), res("cs")], writes=[res("rtmp")])
                    o("vector", lambda e, tb=tb, sl=sl, co=co: e.tensor_tensor(
                        tb[:, sl], tb[:, sl], cs_s[:, co:co + 512], ALU.mult),
                      reads=[res("cs")], writes=[rtb])
                    o("vector", lambda e, tb=tb, sl=sl: e.tensor_tensor(
                        tb[:, sl], tb[:, sl], rtmp[:, sl], ALU.add),
                      reads=[res("rtmp")], writes=[rtb], inc=("v", 1))

            rope(k0, res("k0"), 2048, 0)
            rope(q0, res("q0"), 1024, 1024)
            o("sync", lambda e, hh=hh: e.dma_start(out=k_d[hh], in_=k0[:, :]),
              reads=[res("k0")], writes=[res("k_d")], inc=("sx", 16))
            o("sync", lambda e, hh=hh: e.dma_start(out=q_d[hh], in_=q0[:, :]),
              reads=[res("q0")], writes=[res("q_d")], inc=("sx", 16))
            o("sync", lambda e, hh=hh: e.dma_start(out=v_d[hh], in_=v0[:, :]),
              reads=[res("v0")], writes=[res("v_d")], inc=("sx", 16))

        B.phase = "P3"
        B.barrier()
        def pmn(j):
            return max(0, (j - 8) // 2)

        def pmx(j):
            return min(3, j // 2)

        poffs = []
        acc_off = 0
        for j in range(NT):
            poffs.append(acc_off)
            acc_off += 256 * (pmx(j) - pmn(j) + 1)
        assert acc_off == 10240, acc_off

        def emit_A_j(hh, j):
            buf = hh % 2
            rk, rq = res(f"ks{buf}"), res(f"qs{buf}")
            w = 256 * (pmx(j) - pmn(j) + 1)
            qlo = pmn(j) * 256
            pp, rn0, rn1 = (pAB, "pAB0", "pAB1") if j % 2 == 0 else \
                           (pCD, "pCD0", "pCD1")
            w1 = min(w, 512)
            o("tensor", lambda e, pp=pp, j=j, w1=w1, qlo=qlo, buf=buf:
              e.matmul(pp[:, 0:w1], ks[buf][:, j * 128:(j + 1) * 128],
                       qs[buf][:, qlo:qlo + w1], start=True, stop=True),
              reads=[rk, rq], writes=[res(rn0)],
              inc=("t", 1) if w <= 512 else None)
            if w > 512:
                o("tensor", lambda e, pp=pp, j=j, w=w, qlo=qlo, buf=buf:
                  e.matmul(pp[:, 512:w], ks[buf][:, j * 128:(j + 1) * 128],
                           qs[buf][:, qlo + 512:qlo + w],
                           start=True, stop=True),
                  reads=[rk, rq], writes=[res(rn1)], inc=("t", 1))
            rps = [res(rn0)] + ([res(rn1)] if w > 512 else [])
            po = poffs[j]
            o("scalar", lambda e, pp=pp, w=w, j=j, po=po, buf=buf:
              e.activation(
                pb[buf][:, po:po + w], pp[:, 0:w], AF.Exp,
                bias=exbs[:, j:j + 1], scale=SCALE),
              reads=rps + [res("exb")], writes=[res(f"pb{buf}")],
              inc=("sc", 1))
            for p in range(pmn(j), pmx(j) + 1):
                for sl_i, s in enumerate((2 * p, 2 * p + 1)):
                    ci = j - s
                    off = po + (p - pmn(j)) * 256 + sl_i * 128
                    if ci == 0:
                        mt, mres = atrb, "atri"
                    elif ci == 8:
                        mt, mres = trib, "tri"
                    elif 0 < ci < 8:
                        continue
                    else:
                        o("vector", lambda e, off=off, buf=buf: e.memset(
                            pb[buf][:, off:off + 128], 0.0),
                          writes=[res(f"pb{buf}")], inc=None)
                        continue
                    o("vector", lambda e, off=off, mt=mt, buf=buf:
                      e.tensor_tensor(
                        pb[buf][:, off:off + 128], pb[buf][:, off:off + 128],
                        mt[:, :], ALU.mult),
                      reads=[res(mres)], writes=[res(f"pb{buf}")], inc=None)
            o("vector", lambda e: e.nop(), writes=[res(f"pb{buf}")],
              inc=("v", 1))

        def emit_B_pair(hh, p):
            buf = hh % 2
            vb = hh % 3
            rv = res(f"vs{vb}")
            for idx, j in enumerate(range(2 * p, 2 * p + 10)):
                off = poffs[j] + (p - pmn(j)) * 256
                o("tensor", lambda e, p=p, j=j, off=off, idx=idx, buf=buf,
                  vb=vb: e.matmul(
                    pE[:, p * 256:(p + 1) * 256],
                    vs[vb][:, j * 128:(j + 1) * 128],
                    pb[buf][:, off:off + 256],
                    start=(idx == 0), stop=(idx == 9)),
                  reads=[res(f"pb{buf}"), rv],
                  writes=[res("pE0") if p < 2 else res("pE1")]
                  if idx == 0 else (),
                  inc=None)
                o("tensor", lambda e, p=p, j=j, off=off, idx=idx, buf=buf:
                  e.matmul(
                    pF[0:1, p * 256:(p + 1) * 256], oneb[:, 0:1],
                    pb[buf][:, off:off + 256],
                    start=(idx == 0), stop=(idx == 9)),
                  reads=[res(f"pb{buf}"), res("oneb")],
                  writes=[res("pF0") if p < 2 else res("pF1")]
                  if idx == 0 else (),
                  inc=("t", 1) if idx == 9 else None)

        def emit_tail(hh):
            buf = hh % 2
            if debug:
                o("sync", lambda e, hh=hh: e.dma_start(
                    out=dpb_d, in_=pb[0][:, :]),
                  reads=[res("pb0")], writes=[res("dpb")], inc=("sx", 16)) \
                  if hh == 0 else None
            o("vector", lambda e: e.tensor_copy(denr[0:1, :], pF[0:1, 0:1024]),
              reads=[res("pF0"), res("pF1")], writes=[res("denr")], inc=("v", 1))
            if debug:
                o("sync", lambda e, hh=hh: e.dma_start(
                    out=dden_d[hh], in_=denr[0:1, :]),
                  reads=[res("denr")], writes=[res("dden")], inc=("sx", 16))
            o("vector", lambda e: e.reciprocal(recr[0:1, :], denr[0:1, :]),
              reads=[res("denr")], writes=[res("recr")], inc=("v", 1))
            o("vector", lambda e: e.tensor_copy(rec16[0:1, :], recr[0:1, :]),
              reads=[res("recr")], writes=[res("rec16")], inc=("v", 1))
            o("tensor", lambda e: e.matmul(
                pF[:, 0:512], oneb[0:1, :], rec16[0:1, 0:512],
                start=True, stop=True),
              reads=[res("rec16"), res("oneb")], writes=[res("pF0")], inc=None)
            o("tensor", lambda e: e.matmul(
                pF[:, 512:1024], oneb[0:1, :], rec16[0:1, 512:1024],
                start=True, stop=True),
              reads=[res("rec16"), res("oneb")], writes=[res("pF1")],
              inc=("t", 1))
            o("vector", lambda e: e.tensor_copy(recb[:, :], pF[:, 0:1024]),
              reads=[res("pF0"), res("pF1")], writes=[res("recb")],
              inc=("v", 1))
            if debug:
                o("sync", lambda e, hh=hh: e.dma_start(
                    out=drec_d[hh], in_=recb[:, :]),
                  reads=[res("recb")], writes=[res("drec")], inc=("sx", 16))
            o("vector", lambda e, buf=buf: e.tensor_tensor(
                atts[buf][:, :], pE[:, 0:1024], recb[:, :], ALU.mult),
              reads=[res("pE0"), res("pE1"), res("recb")],
              writes=[res(f"atts{buf}")], inc=("v", 1))
            o("sync", lambda e, hh=hh, buf=buf: e.dma_start(
                out=att_d[hh], in_=atts[buf][:, :]),
              reads=[res(f"atts{buf}")], writes=[res("att_d")], inc=("sx", 16))

        ld("sync", "sx", ks[0][:, :], k_d[0], res("ks0"))
        ld("sync", "sx", vs[0][:, :], v_d[0], res("vs0"))
        ld("sync", "sx", qs[0][:, :], q_d[0], res("qs0"))
        for hh in range(NH):
            if hh + 1 < NH:
                nb = (hh + 1) % 2
                vb = (hh + 1) % 3
                ld("sync", "sx", ks[nb][:, :], k_d[hh + 1], res(f"ks{nb}"))
                ld("sync", "sx", vs[vb][:, :], v_d[hh + 1], res(f"vs{vb}"))
                ld("sync", "sx", qs[nb][:, :], q_d[hh + 1], res(f"qs{nb}"))
            for j in range(NT):
                emit_A_j(hh, j)
                if hh > 0 and j % 4 == 3:
                    emit_B_pair(hh - 1, j // 4)
            if hh > 0:
                emit_tail(hh - 1)
        for p in range(4):
            emit_B_pair(NH - 1, p)
        emit_tail(NH - 1)

        B.phase = "P4"
        B.barrier()
        o("sync", lambda e: e.dma_start(
            out=attV[:, :, :], in_=att_d.rearrange("h p q -> p h q")),
          reads=[res("att_d")], writes=[res("attR")], inc=("sx", 16))
        for os_ in range(8):
            wb = os_ % 2
            o("gpsimd", lambda e, os_=os_, wb=wb: e.dma_start(
                out=woH[wb], in_=wo_d[:, :, os_ * 512:(os_ + 1) * 512]),
              writes=[res(f"wo{wb}")], inc=("sg", 16))
            for qt in range(QT):
                pn = ["pAB0", "pAB1", "pCD0", "pCD1"][qt % 4]
                pp = {"pAB0": pAB[:, 0:512], "pAB1": pAB[:, 512:1024],
                      "pCD0": pCD[:, 0:512], "pCD1": pCD[:, 512:1024]}[pn]
                for c in range(NC):
                    o("tensor", lambda e, pp=pp, c=c, qt=qt, wb=wb: e.matmul(
                        pp, attV[:, c, qt * 128:(qt + 1) * 128],
                        woH[wb][:, c, :], start=(c == 0), stop=(c == NC - 1)),
                      reads=[res("attR"), res(f"wo{wb}")],
                      writes=[res(pn)] if c == 0 else (),
                      inc=("t", 1) if c == NC - 1 else None)
                mb = qt % 2
                o("sync", lambda e, os_=os_, qt=qt, mb=mb: e.dma_start(
                    out=xms[mb],
                    in_=xmid_d[1024 + qt * 128:1024 + (qt + 1) * 128,
                               os_ * 512:(os_ + 1) * 512]),
                  reads=[res("xmid_d")], writes=[res(f"xms{mb}")],
                  inc=("sx", 16))
                o("vector", lambda e, pp=pp, mb=mb: e.tensor_tensor(
                    xps[mb], xms[mb], pp, ALU.add),
                  reads=[res(pn), res(f"xms{mb}")], writes=[res(f"xps{mb}")],
                  inc=("v", 1))
                o("sync", lambda e, os_=os_, qt=qt, mb=mb: e.dma_start(
                    out=xpost_d[qt * 128:(qt + 1) * 128,
                                os_ * 512:(os_ + 1) * 512], in_=xps[mb]),
                  reads=[res(f"xps{mb}")], writes=[res("xpost_d")],
                  inc=("sx", 16))

        B.phase = "P5"
        B.barrier()
        ld("gpsimd", "sg", rAs[:, :], ra_d.rearrange("p c e -> p (c e)"),
           res("rAs"))
        ld("gpsimd", "sg", bcs[0:64, :], bc_d, res("bcs"))
        for qt in range(QT):
            xr = xrs[qt % 2]
            rxr = res(f"xr{qt % 2}")
            o("sync", lambda e, qt=qt, xr=xr: e.dma_start(
                out=xr, in_=xpost_d[qt * 128:(qt + 1) * 128, :]),
              reads=[res("xpost_d")], writes=[rxr], inc=("sx", 16))
            ln_stats(xr, rxr)
            o("vector", lambda e, xr=xr: e.tensor_scalar(
                xntm, xr, mcol, rstd, ALU.subtract, ALU.mult),
              reads=[rxr, res("stt")], writes=[res("xntm")], inc=("v", 1))
            transpose_set(xntm, res("xntm"),
                          lambda e, g, src: e.tensor_copy(
                              xnfm[:, g * 1024:(g + 1) * 1024], src),
                          res("xnfm"),
                          alt_dst=lambda g: xnfm[:, g * 1024:(g + 1) * 1024])
            ffn_tile(xnfm, res("xnfm"), xr, rxr)
            o("sync", lambda e, qt=qt, xr=xr: e.dma_start(
                out=out_d[qt * 128:(qt + 1) * 128, :], in_=xr),
              reads=[rxr], writes=[res("out_d")], inc=("sx", 16))

        fin = B.val("sx")
        B.ops.append(("sync", lambda e: e.nop(), [("sx", fin)], []))
        fin_g = B.val("sg")
        B.ops.append(("gpsimd", lambda e: e.nop(), [("sg", fin_g)], []))
        fin_b = B.val("sb")
        B.ops.append(("scalar", lambda e: e.nop(), [("sb", fin_b)], []))

        def mk(eng_name):
            def body(e):
                for en, fn, waits, incs in B.ops:
                    if en != eng_name:
                        continue
                    for s, vv in waits:
                        e.wait_ge(sems[s], vv)
                    instr = fn(e)
                    for s, amt in incs:
                        instr.then_inc(sems[s], amt)
            body.__name__ = f"_{eng_name}"
            return body

        block.tensor(mk("tensor"))
        block.vector(mk("vector"))
        block.scalar(mk("scalar"))
        block.sync(mk("sync"))
        block.gpsimd(mk("gpsimd"))

    return nc



def _prep_core_inputs(x_ctx, h, Wq, Wk, Wv, Wo, rw, A, Bm):
    D_ = 4096
    bf = ml_dtypes.bfloat16

    def blk_qkv(W):
        return np.ascontiguousarray(
            W.T.reshape(32, 128, 32, 128).transpose(2, 1, 0, 3)
            .reshape(32, 128, 4096))

    wo_blk = np.ascontiguousarray(
        Wo.T.reshape(32, 128, D_).transpose(1, 0, 2))
    acat = A.reshape(64, D_)
    rA = np.concatenate([acat.T, rw.T], axis=1)
    ra_blk = np.ascontiguousarray(rA.reshape(32, 128, 68).transpose(1, 0, 2))
    bcat = np.ascontiguousarray(Bm.transpose(0, 2, 1).reshape(64, D_))

    pos = (np.arange(2048) - 1024 * (1 - h)).astype(np.float64)
    inv = 1.0 / 10000.0 ** (np.arange(0, 128, 2, dtype=np.float64) / 128)
    fr = pos[None, :] * inv[:, None]
    cos = np.concatenate([np.cos(fr), np.cos(fr)], axis=0)
    sin = np.concatenate([-np.sin(fr), np.sin(fr)], axis=0)
    cs = np.concatenate([cos, sin], axis=1).astype(np.float32)

    kk = np.arange(128)[:, None]
    rr = np.arange(128)[None, :]
    tri = (kk <= rr).astype(bf)
    atri = (kk > rr).astype(bf)
    expb = np.full((128, 16), SHIFT, np.float32)
    if h == 0:
        expb[:, :8] = NEGB

    return {
        "x": np.ascontiguousarray(x_ctx, dtype=np.float32),
        "wq": blk_qkv(Wq), "wk": blk_qkv(Wk), "wv": blk_qkv(Wv),
        "wo": wo_blk,
        "ra": ra_blk.astype(np.float32),
        "bc": bcat.astype(np.float32),
        "cs": cs,
        "identb": np.eye(128, dtype=bf),
        "identf": np.eye(128, dtype=np.float32),
        "pswap": np.roll(np.eye(128), 64, axis=0).astype(bf),
        "onesb": np.ones((128, 128), dtype=bf),
        "tri": tri, "atri": atri, "expb": expb,
    }


def kernel(L0_attn_x, L0_attn_Wq, L0_attn_Wk, L0_attn_Wv, L0_attn_Wo,
           L0_ffn_router_w, L0_ffn_A, L0_ffn_B):
    x = np.asarray(L0_attn_x, dtype=np.float32)
    Wq = np.asarray(L0_attn_Wq, np.float32)
    Wk = np.asarray(L0_attn_Wk, np.float32)
    Wv = np.asarray(L0_attn_Wv, np.float32)
    Wo = np.asarray(L0_attn_Wo, np.float32)
    rw = np.asarray(L0_ffn_router_w, np.float32)
    A = np.asarray(L0_ffn_A, np.float32)
    Bm = np.asarray(L0_ffn_B, np.float32)

    nc = build_program(h=0)
    in_maps = []
    for c in range(8):
        b, h = c // 2, c % 2
        if h == 0:
            x_ctx = np.concatenate(
                [np.zeros((1024, 4096), np.float32), x[b, :1024]], axis=0)
        else:
            x_ctx = x[b]
        in_maps.append(_prep_core_inputs(x_ctx, h, Wq, Wk, Wv, Wo, rw, A, Bm))

    result = run_bass_kernel_spmd(nc, in_maps, core_ids=list(range(8)))
    out = np.empty((4, 2048, 4096), np.float32)
    for c in range(8):
        b, h = c // 2, c % 2
        out[b, h * 1024:(h + 1) * 1024] = result.results[c]["out"]
    return out
